# version 03
# baseline (speedup 1.0000x reference)
import contextlib
import numpy as np
import ml_dtypes
import concourse.bass as bass
import concourse.mybir as mybir
from concourse.bass_utils import run_bass_kernel_spmd

F32 = mybir.dt.float32
BF16 = mybir.dt.bfloat16
I32 = mybir.dt.int32
AF = mybir.ActivationFunctionType
ALU = mybir.AluOpType
AX = mybir.AxisListType

L = 2048
DM = 2048
INTER = 4096
CONVCH = 33
PROJP = 8448
NM_IP = 66
NM_F1 = 64
NK = 16
T = 512
NPASS = 4
NCH = 4
CH = 128
DFF = 8192

ENGINES = ("sync", "scalar", "vector", "tensor", "gpsimd")


class Emitter:
    def __init__(self, nc, sem_names):
        self.nc = nc
        self.q = {e: [] for e in ENGINES}
        self.cnt = {s: 0 for s in sem_names}
        self.sems = {}
        self.bankR = {}

    def op(self, eng, fn, wait=(), inc=None, k=1):
        w = [(s, c) for (s, c) in wait if c > 0]
        if inc is not None:
            self.cnt[inc] += k
        self.q[eng].append((fn, tuple(w), inc, k))
        return self.cnt[inc] if inc is not None else None

    def n(self, s):
        return self.cnt[s]

    def bank_write_waits(self, bank):
        return list(self.bankR.get(bank, {}).items())

    def bank_read_mark(self, bank, sem):
        d = self.bankR.setdefault(bank, {})
        d[sem] = max(d.get(sem, 0), self.cnt[sem])

    def finalize(self, block):
        for eng in ENGINES:
            ops = self.q[eng]

            def make(eng_ops):
                def body(e):
                    seen = {}
                    for fn, waits, inc, k in eng_ops:
                        for s, c in waits:
                            if seen.get(s, -1) < c:
                                e.wait_ge(self.sems[s], c)
                                seen[s] = c
                        ins = fn(e) if fn is not None else None
                        if inc is not None:
                            ins.then_inc(self.sems[inc], k)
                return body

            getattr(block, eng)(make(ops))


def build(dbg=False):
    nc = bass.Bass("TRN2", target_bir_lowering=False, debug=False)

    x_d = nc.dram_tensor("x", [L, DM], F32, kind="ExternalInput").ap()
    wip_d = nc.dram_tensor("wip", [NM_IP, 128, NK * 128], F32, kind="ExternalInput").ap()
    w1_d = nc.dram_tensor("w1t", [NM_F1, 128, NK * 128], F32, kind="ExternalInput").ap()
    w2_d = nc.dram_tensor("w2t", [DFF, DM], F32, kind="ExternalInput").ap()
    wo_d = nc.dram_tensor("wot", [INTER, DM], F32, kind="ExternalInput").ap()
    convw_d = nc.dram_tensor("convw", [128, CONVCH * 4], F32, kind="ExternalInput").ap()
    convb_d = nc.dram_tensor("convb", [128, CONVCH], F32, kind="ExternalInput").ap()
    dtb_d = nc.dram_tensor("dtb", [64, 1], F32, kind="ExternalInput").ap()
    alog_d = nc.dram_tensor("alog", [64, 1], F32, kind="ExternalInput").ap()
    dvec_d = nc.dram_tensor("dvec", [1, 64], F32, kind="ExternalInput").ap()
    nwc_d = nc.dram_tensor("nwc", [128, 32], F32, kind="ExternalInput").ap()
    identf_d = nc.dram_tensor("identf", [128, 128], F32, kind="ExternalInput").ap()
    identb_d = nc.dram_tensor("identb", [128, 128], BF16, kind="ExternalInput").ap()
    nmask_d = nc.dram_tensor("nmask", [128, 128], F32, kind="ExternalInput").ap()
    onesf_d = nc.dram_tensor("onesf", [1, 128], F32, kind="ExternalInput").ap()

    out_d = nc.dram_tensor("out", [L, DM], F32, kind="ExternalOutput").ap()

    cip_d = nc.dram_tensor("cache_ip", [NM_IP, 128, NK * 128], BF16).ap()
    cw1_d = nc.dram_tensor("cache_w1", [NM_F1, 128, NK * 128], BF16).ap()
    cw2_d = nc.dram_tensor("cache_w2", [DFF, DM], BF16).ap()
    cwo_d = nc.dram_tensor("cache_wo", [INTER, DM], BF16).ap()

    dbg_t = {}
    if dbg:
        for name, shape, dt in [
            ("d_xn", [128, NK * T], BF16), ("d_xq", [128, NK * T], BF16),
            ("d_gate", [128, 2 * T], BF16), ("d_xcv", [128, 2 * T], BF16),
            ("d_bc", [128, T], BF16), ("d_dt", [64, T], F32),
            ("d_ca", [64, CH], F32), ("d_cat", [128, 64], F32),
            ("d_ecat", [128, 64], F32), ("d_lrep", [64, 64], F32),
            ("d_wtm", [128, 64], F32), ("d_mt", [128, 128], F32),
            ("d_xt", [128, INTER], BF16), ("d_dtx", [128, INTER], BF16),
            ("d_ytm", [128, INTER], BF16), ("d_yg", [128, 2 * T], BF16),
            ("d_s", [64, INTER], BF16), ("d_rms", [128, 4], F32),
            ("d_sc1", [128, 8], F32), ("d_sc2", [128, 8], F32),
            ("d_h", [128, 2 * T], BF16), ("d_hq", [128, 2 * T], BF16),
            ("d_gtb", [128, 128], BF16), ("d_r1b", [128, T], F32),
            ("d_sacc", [128, 256], F32), ("d_ss2", [1, 8], F32),
            ("d_xcvf", [128, 32 * T], BF16), ("d_gatef", [128, 32 * T], BF16),
            ("d_ps0", [128, T], F32), ("d_ps2", [128, T], F32),
        ]:
            dbg_t[name] = nc.dram_tensor(name, shape, dt, kind="ExternalOutput").ap()

    sem_names = ["dW", "dX", "dC", "dO", "dS", "dT", "T", "V", "S"]
    em = Emitter(nc, sem_names)

    with contextlib.ExitStack() as cx:
        E = cx.enter_context
        XN = E(nc.sbuf_tensor("XN", [128, NK * T], BF16))
        XQ = E(nc.sbuf_tensor("XQ", [128, NK * T], BF16))
        H = E(nc.sbuf_tensor("H", [128, 64 * T], BF16))
        YG = E(nc.sbuf_tensor("YG", [128, 32 * T], BF16))
        YTM = E(nc.sbuf_tensor("YTM", [128, INTER], BF16))
        S_ = E(nc.sbuf_tensor("S", [64, INTER], BF16))
        WB = E(nc.sbuf_tensor("WB", [128, 6144], BF16))
        XIN = E(nc.sbuf_tensor("XIN", [128, 2048], F32))
        SCR = E(nc.sbuf_tensor("SCR", [128, 2048], F32))
        TMB = E(nc.sbuf_tensor("TMB", [128, 2048], BF16))
        CIN = E(nc.sbuf_tensor("CIN", [128, 516], F32))
        M1 = E(nc.sbuf_tensor("M1", [128, T], F32))
        BCb = E(nc.sbuf_tensor("BCb", [128, T], BF16))
        C0b = E(nc.sbuf_tensor("C0b", [64, T], BF16))
        HALO = E(nc.sbuf_tensor("HALO", [128, 128], F32))
        DT = E(nc.sbuf_tensor("DT", [64, T], F32))
        AAC = E(nc.sbuf_tensor("AAC", [64, CH], F32))
        A2 = E(nc.sbuf_tensor("A2", [64, T], F32))
        WALL = A2[:, 0:CH]
        LTMP = A2[:, CH:CH + 64]
        CA = E(nc.sbuf_tensor("CA", [64, CH], F32))
        CAT = E(nc.sbuf_tensor("CAT", [128, 64], F32))
        NCAT = E(nc.sbuf_tensor("NCAT", [128, 64], F32))
        WTM = E(nc.sbuf_tensor("WTM", [128, 64], F32))
        LREP = E(nc.sbuf_tensor("LREP", [64, 64], F32))
        ECAT = E(nc.sbuf_tensor("ECAT", [128, 64], F32))
        MT = E(nc.sbuf_tensor("MT", [128, 128], F32))
        GSB = E(nc.sbuf_tensor("GSB", [128, 512], F32))
        QTB = E(nc.sbuf_tensor("QTB", [128, 512], F32))
        GTB = E(nc.sbuf_tensor("GTB", [128, 1024], BF16))
        BW = E(nc.sbuf_tensor("BW", [128, 512], BF16))
        BTM = E(nc.sbuf_tensor("BTM", [128, 64], BF16))
        DTTM = E(nc.sbuf_tensor("DTTM", [128, 64], F32))
        IDF = E(nc.sbuf_tensor("IDF", [128, 128], F32))
        IDB = E(nc.sbuf_tensor("IDB", [128, 128], BF16))
        NMK = E(nc.sbuf_tensor("NMK", [128, 128], F32))
        ONEF = E(nc.sbuf_tensor("ONEF", [1, 128], F32))
        ONEC = E(nc.sbuf_tensor("ONEC", [128, 1], F32))
        ONET = E(nc.sbuf_tensor("ONET", [64, 128], F32))
        CW = E(nc.sbuf_tensor("CW", [128, CONVCH * 4], F32))
        CB = E(nc.sbuf_tensor("CB", [128, CONVCH], F32))
        DTB = E(nc.sbuf_tensor("DTB", [64, 1], F32))
        ALOG = E(nc.sbuf_tensor("ALOG", [64, 1], F32))
        ACOL = E(nc.sbuf_tensor("ACOL", [64, 1], F32))
        DVEC = E(nc.sbuf_tensor("DVEC", [1, 64], F32))
        DREP = E(nc.sbuf_tensor("DREP", [128, 64], F32))
        NWC = E(nc.sbuf_tensor("NWC", [128, 32], F32))
        CE5 = E(nc.sbuf_tensor("CE5", [128, 1], F32))
        NE5 = E(nc.sbuf_tensor("NE5", [128, 1], F32))
        CL127 = E(nc.sbuf_tensor("CL127", [128, 1], F32))
        CE6 = E(nc.sbuf_tensor("CE6", [128, 1], F32))
        CONE = E(nc.sbuf_tensor("CONE", [64, 1], F32))
        SACC = E(nc.sbuf_tensor("SACC", [128, 256], F32))
        SS2 = E(nc.sbuf_tensor("SS2", [1, 8], F32))
        SROW = E(nc.sbuf_tensor("SROW", [1, 512], F32))
        RMS = E(nc.sbuf_tensor("RMS", [128, 4], F32))
        AM = E(nc.sbuf_tensor("AM", [128, 8], F32))
        SC1 = E(nc.sbuf_tensor("SC1", [128, 8], F32))
        SC2C = E(nc.sbuf_tensor("SC2C", [128, 8], F32))
        TMPC = E(nc.sbuf_tensor("TMPC", [128, 8], F32))
        SUMC = E(nc.sbuf_tensor("SUMC", [128, 8], F32))
        LNC = E(nc.sbuf_tensor("LNC", [128, 8], F32))
        RB2 = E(nc.sbuf_tensor("RB2", [128, 1024], F32))

        pb = [E(nc.psum_tensor(f"pb{i}", [128, 512], F32)) for i in range(6)]
        pt = [E(nc.psum_tensor(f"pt{i}", [128, 1024], BF16)) for i in range(2)]
        ptf = [p[:].bitcast(F32) for p in pt]

        for s in sem_names:
            em.sems[s] = E(nc.semaphore(s))
        block = E(nc.Block())

        XT = XN[:, 0:INTER]
        DTX = XN[:, INTER:2 * INTER]
        R1B = RB2[:, 0:512]
        S2B = RB2[:, 0:512]
        R2B = RB2[:, 512:1024]

        o = em.op
        n = em.n
        bw = em.bank_write_waits
        br = em.bank_read_mark

        def dma(dst, src):
            return lambda e: e.dma_start(out=dst, in_=src)

        def dump(name, src_ap, waits, barrier=True):
            if not dbg:
                return
            o("gpsimd", dma(dbg_t[name], src_ap), wait=waits, inc="dO", k=16)
            mk = n("dO")
            if barrier:
                for eng in ("vector", "tensor", "scalar"):
                    o(eng, None, wait=[("dO", mk)])

        for dst, src in [(IDF[:], identf_d[:]), (IDB[:], identb_d[:]), (NMK[:], nmask_d[:]),
                         (ONEF[:], onesf_d[:]), (CW[:], convw_d[:]), (CB[:], convb_d[:]),
                         (DTB[:], dtb_d[:]), (ALOG[:], alog_d[:]), (DVEC[:], dvec_d[:]),
                         (NWC[:], nwc_d[:])]:
            o("sync", dma(dst, src), inc="dS", k=16)
        k_consts = n("dS")

        for ms in [lambda v: v.memset(CE5[:], 1e-5), lambda v: v.memset(CE6[:], 1e-6),
                   lambda v: v.memset(NE5[:], -1e-5),
                   lambda v: v.memset(CL127[:], float(np.log(np.float64(127.0)))),
                   lambda v: v.memset(CONE[:], 1.0), lambda v: v.memset(ONET[:], 1.0),
                   lambda v: v.memset(ONEC[:], 1.0), lambda v: v.memset(HALO[:], 0.0),
                   lambda v: v.memset(S_[:], 0.0), lambda v: v.memset(SACC[:], 0.0)]:
            o("vector", ms)
        o("vector", lambda v: v.drain(), inc="V")
        v_memset = n("V")

        o("scalar", lambda s: s.activation(out=ACOL[:], in_=ALOG[:], func=AF.Exp),
          wait=[("dS", k_consts)], inc="S")
        o("vector", lambda v: v.tensor_scalar_mul(ACOL[:], ACOL[:], -1.0),
          wait=[("S", n("S"))])
        o("vector", lambda v: v.drain(), inc="V")

        o("tensor", lambda t: t.matmul(pb[0][:, 0:64], ONEF[:], DVEC[:], start=True, stop=True),
          wait=[("dS", k_consts)] + bw("pb0"), inc="T")
        o("vector", lambda v: v.tensor_copy(out=DREP[:], in_=pb[0][:, 0:64]),
          wait=[("T", n("T"))], inc="V")
        br("pb0", "V")

        stat_list = []
        for i in range(128):
            m, hf = i // 2, i % 2
            stat_list.append((w1_d[m][:, hf * 1024:(hf + 1) * 1024], i))
        for i in range(128):
            rc, hf = i // 2, i % 2
            stat_list.append((w2_d[rc * 128:(rc + 1) * 128, hf * 1024:(hf + 1) * 1024], 128 + i))
        stat_state = {"i": 0, "marks": {}, "first_guard": []}

        def emit_stat_ops(nops):
            for _ in range(nops):
                i = stat_state["i"]
                if i >= len(stat_list):
                    return
                stat_state["i"] += 1
                src, col = stat_list[i]
                half = i % 2
                sl = SCR[:, half * 1024:(half + 1) * 1024]
                acc = SACC[:, col:col + 1]
                wdep = list(stat_state["first_guard"]) if i < 2 else []
                if i - 2 in stat_state["marks"]:
                    wdep.append(stat_state["marks"][i - 2])
                o("gpsimd", dma(sl, src), wait=wdep, inc="dT", k=16)
                dcnt = n("dT")
                if i % 2 == 0:
                    o("scalar", lambda s, sl=sl, acc=acc: s.activation(
                        out=sl, in_=sl, func=AF.Square, accum_out=acc),
                      wait=[("dT", dcnt), ("V", v_memset)], inc="S")
                    stat_state["marks"][i] = ("S", n("S"))
                else:
                    o("vector", lambda v, sl=sl, acc=acc: v.scalar_tensor_tensor(
                        out=sl, in0=sl, scalar=1.0, in1=sl, op0=ALU.mult, op1=ALU.mult,
                        accum_out=acc),
                      wait=[("dT", dcnt), ("V", v_memset)], inc="V")
                    stat_state["marks"][i] = ("V", n("V"))

        xin_free = []
        scr_free = []
        tmb_freeT = 0
        t_wb_free = 0
        c0b_mark = 0
        c0b_freeT = 0
        v_rms = 0
        dC_mark = {}

        for p in range(NPASS):
            for tc in range(NCH):
                g = p * NCH + tc
                o("scalar", dma(XIN[:], x_d[g * CH:(g + 1) * CH, :]),
                  wait=xin_free, inc="dX", k=16)
                d_x = n("dX")
                o("vector", lambda v: v.tensor_reduce(
                    out=SUMC[:, 0:1], in_=XIN[:], axis=AX.X, op=ALU.add),
                  wait=[("dX", d_x)] + scr_free, inc="V")
                v_nm = n("V")
                o("scalar", lambda s: s.activation(out=TMPC[:, 0:1], in_=SUMC[:, 0:1],
                                                   func=AF.Identity, scale=-1.0 / DM),
                  wait=[("V", v_nm)])
                o("scalar", lambda s: s.drain())
                o("scalar", lambda s: s.activation(
                    out=TMB[:],
                    in_=XIN[:], func=AF.Square, bias=TMPC[:, 0:1],
                    accum_out=SUMC[:, 1:2]),
                  wait=[("T", tmb_freeT)])
                o("scalar", lambda s: s.activation(out=LNC[:, 0:1], in_=SUMC[:, 1:2],
                                                   func=AF.Ln, scale=1.0 / DM, bias=CE5[:]))
                o("scalar", lambda s: s.activation(out=LNC[:, 0:1], in_=LNC[:, 0:1],
                                                   func=AF.Exp, scale=-0.5), inc="S")
                s_rs = n("S")
                o("vector", lambda v: v.tensor_scalar(
                    out=SCR[:], in0=XIN[:], scalar1=TMPC[:, 0:1], scalar2=LNC[:, 0:1],
                    op0=ALU.add, op1=ALU.mult),
                  wait=[("S", s_rs)], inc="V")
                xin_free = [("V", n("V")), ("S", s_rs)]
                o("vector", lambda v: v.tensor_copy(out=TMB[:], in_=SCR[:]),
                  wait=[("T", tmb_freeT), ("S", s_rs)], inc="V")
                v_xnb = n("V")
                o("vector", lambda v, tc=tc: v.tensor_reduce(
                    out=AM[:, tc:tc + 1], in_=SCR[:], axis=AX.X, op=ALU.max,
                    apply_absolute_value=True), inc="V")
                v_am = n("V")
                o("scalar", lambda s, tc=tc: s.activation(out=TMPC[:, 1:2],
                                                          in_=AM[:, tc:tc + 1],
                                                          func=AF.Relu, bias=NE5[:]),
                  wait=[("V", v_am)])
                o("scalar", lambda s: s.drain())
                o("scalar", lambda s: s.activation(out=TMPC[:, 1:2], in_=TMPC[:, 1:2],
                                                   func=AF.Ln, bias=CE5[:]))
                o("scalar", lambda s: s.drain())
                o("scalar", lambda s, tc=tc: s.activation(out=SC1[:, tc:tc + 1],
                                                          in_=TMPC[:, 1:2], func=AF.Exp,
                                                          scale=-1.0, bias=CL127[:]),
                  inc="S")
                s_sc1 = n("S")
                for grp in range(2):
                    kb = grp * 8
                    bname = "pt%d" % grp
                    bank = pt[grp]
                    for kk in range(8):
                        k = kb + kk
                        o("tensor", lambda t, k=k, kk=kk, bank=bank: t.transpose(
                            bank[:, kk * 128:(kk + 1) * 128], TMB[:, k * 128:(k + 1) * 128],
                            IDB[:]),
                          wait=([("V", v_xnb), ("dS", k_consts)] + bw(bname)) if kk == 0 else (),
                          inc="T" if kk == 7 else None)
                    t_fill = n("T")

                    def evac(v, kb=kb, tc=tc, bank=bank, dstbuf=XN):
                        dv = dstbuf.rearrange("p (k t) -> p k t", k=NK)[
                            :, kb:kb + 8, tc * CH:(tc + 1) * CH]
                        sv = bank.rearrange("p (k t) -> p k t", k=8)
                        return v.tensor_copy(out=dv, in_=sv)
                    o("vector", evac, wait=[("T", t_fill)], inc="V")
                    br(bname, "V")
                t_xn_done = n("T")
                o("vector", lambda v, tc=tc: v.tensor_scalar(
                    out=SCR[:].bitcast(I32), in0=SCR[:], scalar1=SC1[:, tc:tc + 1],
                    scalar2=None, op0=ALU.mult),
                  wait=[("S", s_sc1)])
                o("vector", lambda v: v.tensor_copy(out=TMB[:], in_=SCR[:].bitcast(I32)),
                  wait=[("T", t_xn_done)], inc="V")
                v_xqb = n("V")
                for grp in range(2):
                    kb = grp * 8
                    bname = "pt%d" % grp
                    bank = pt[grp]
                    for kk in range(8):
                        k = kb + kk
                        o("tensor", lambda t, k=k, kk=kk, bank=bank: t.transpose(
                            bank[:, kk * 128:(kk + 1) * 128], TMB[:, k * 128:(k + 1) * 128],
                            IDB[:]),
                          wait=([("V", v_xqb)] + bw(bname)) if kk == 0 else (),
                          inc="T" if kk == 7 else None)
                    t_fill = n("T")

                    def evac(v, kb=kb, tc=tc, bank=bank, dstbuf=XQ):
                        dv = dstbuf.rearrange("p (k t) -> p k t", k=NK)[
                            :, kb:kb + 8, tc * CH:(tc + 1) * CH]
                        sv = bank.rearrange("p (k t) -> p k t", k=8)
                        return v.tensor_copy(out=dv, in_=sv)
                    o("vector", evac, wait=[("T", t_fill)], inc="V")
                    br(bname, "V")
                tmb_freeT = n("T")
            vA = n("V")
            if p == 0:
                stat_state["first_guard"] = [("V", vA)]
                dump("d_xn", XN[:], [("V", vA)])
                dump("d_xq", XQ[:], [("V", vA)])

            ip_mm = {-2: t_wb_free, -1: t_wb_free}
            conv_silu_S = 0
            for m in range(NM_IP):
                half = m % 2
                wbh = WB[:, half * 2048:(half + 1) * 2048]
                wt_guard = [("T", ip_mm[m - 2])]
                if p == 0:
                    wde = [dC_mark[("ip", m - 3)]] if ("ip", m - 3) in dC_mark else []
                    o("gpsimd", dma(wbh, wip_d[m]), wait=wt_guard + wde, inc="dT", k=16)
                    w_ready = [("dT", n("dT"))]
                    o("gpsimd", dma(cip_d[m], wbh), wait=w_ready, inc="dC", k=16)
                    dC_mark[("ip", m)] = ("dC", n("dC"))
                else:
                    o("sync", dma(wbh, cip_d[m]), wait=wt_guard, inc="dW", k=16)
                    w_ready = [("dW", n("dW"))]
                bname = "pb%d" % (m % 4)
                bank = pb[m % 4]
                for k in range(NK):
                    o("tensor", lambda t, bank=bank, wbh=wbh, k=k: t.matmul(
                        bank[:], wbh[:, k * 128:(k + 1) * 128],
                        XN[:, k * T:(k + 1) * T], start=(k == 0), stop=(k == NK - 1)),
                      wait=(w_ready + bw(bname) + [("V", vA)]) if k == 0 else (),
                      inc="T" if k == NK - 1 else None)
                ip_mm[m] = n("T")
                tm = ip_mm[m]
                if m < 32:
                    o("scalar", lambda s, m=m, bank=bank: s.activation(
                        out=H[:, m * T:(m + 1) * T], in_=bank[:], func=AF.Silu),
                      wait=[("T", tm)], inc="S")
                    br(bname, "S")
                elif m < 65:
                    ch = m - 32
                    o("vector", lambda v, bank=bank: v.tensor_copy(out=CIN[:, 3:515], in_=bank[:]),
                      wait=[("T", tm), ("S", conv_silu_S)], inc="V")
                    br(bname, "V")
                    o("vector", lambda v, ch=ch: v.tensor_copy(
                        out=CIN[:, 0:3], in_=HALO[:, ch * 3:ch * 3 + 3]))
                    o("vector", lambda v, ch=ch: v.tensor_copy(
                        out=HALO[:, ch * 3:ch * 3 + 3], in_=CIN[:, 512:515]))
                    for kk in range(4):
                        in1 = (CB[:, ch:ch + 1].broadcast_to([128, 512]) if kk == 0 else M1[:])
                        o("vector", lambda v, kk=kk, ch=ch, in1=in1: v.scalar_tensor_tensor(
                            out=M1[:], in0=CIN[:, kk:kk + 512],
                            scalar=CW[:, ch * 4 + kk:ch * 4 + kk + 1],
                            in1=in1, op0=ALU.mult, op1=ALU.add),
                          inc="V" if kk == 3 else None)
                    v_cv = n("V")
                    if ch < 32:
                        o("scalar", lambda s, ch=ch: s.activation(
                            out=H[:, (32 + ch) * T:(33 + ch) * T], in_=M1[:], func=AF.Silu),
                          wait=[("V", v_cv)], inc="S")
                    else:
                        o("scalar", lambda s: s.activation(out=BCb[:], in_=M1[:], func=AF.Silu),
                          wait=[("V", v_cv)], inc="S")
                        o("vector", lambda v: v.tensor_copy(out=BCb[:, 0:1], in_=BCb[:, 0:1]),
                          wait=[("S", n("S"))], inc="V")
                        o("gpsimd", dma(C0b[:], BCb[64:128, :]),
                          wait=[("V", n("V")), ("T", c0b_freeT)], inc="dT", k=16)
                        c0b_mark = n("dT")
                    conv_silu_S = n("S")
                else:
                    o("scalar", lambda s, bank=bank: s.activation(
                        out=DT[:], in_=bank[0:64, :], func=AF.Identity, bias=DTB[:]),
                      wait=[("T", tm)], inc="S")
                    br(bname, "S")
                    o("scalar", lambda s: s.activation(out=A2[:], in_=DT[:], func=AF.Abs))
                    o("scalar", lambda s: s.activation(out=A2[:], in_=A2[:], func=AF.Exp,
                                                       scale=-1.0))
                    o("scalar", lambda s: s.activation(out=A2[:], in_=A2[:], func=AF.Ln,
                                                       bias=CONE[:]))
                    o("scalar", lambda s: s.activation(out=DT[:], in_=DT[:], func=AF.Relu),
                      inc="S")
                    o("vector", lambda v: v.tensor_add(DT[:], DT[:], A2[:]),
                      wait=[("S", n("S"))], inc="V")
                if p == 0:
                    emit_stat_ops(4)
            t_wb_free = ip_mm[NM_IP - 1]
            vB = n("V")
            sB = n("S")
            if p == 0:
                dump("d_xcvf", H[:, 32 * T:64 * T], [("S", sB), ("V", vB)])
                dump("d_gatef", H[:, 0:32 * T], [("S", sB), ("V", vB)])
                dump("d_gate", H[:, 0:2 * T], [("S", sB), ("V", vB)])
                dump("d_xcv", H[:, 32 * T:34 * T], [("S", sB), ("V", vB)])
                dump("d_bc", BCb[:], [("S", sB), ("V", vB)])
                dump("d_dt", DT[:], [("S", sB), ("V", vB)])

            if p == 0:
                emit_stat_ops(len(stat_list))
                sv = n("S")
                o("vector", lambda v: v.tensor_reduce(out=TMPC[:, 4:5], in_=SACC[:, 0:128],
                                                      axis=AX.X, op=ALU.add),
                  wait=[("S", sv)])
                o("vector", lambda v: v.tensor_reduce(out=TMPC[:, 5:6], in_=SACC[:, 128:256],
                                                      axis=AX.X, op=ALU.add))
                o("vector", lambda v: v.drain(), inc="V")
                o("tensor", lambda t: t.transpose(pb[4][0:1, 0:128], TMPC[:, 4:5], IDF[:]),
                  wait=[("V", n("V"))] + bw("pb4"), inc="T")
                o("tensor", lambda t: t.transpose(pb[4][0:1, 128:256], TMPC[:, 5:6], IDF[:]),
                  inc="T")
                o("vector", lambda v: v.tensor_copy(out=SROW[0:1, 0:256], in_=pb[4][0:1, 0:256]),
                  wait=[("T", n("T"))], inc="V")
                br("pb4", "V")
                o("vector", lambda v: v.tensor_reduce(out=SS2[:, 0:1], in_=SROW[0:1, 0:128],
                                                      axis=AX.X, op=ALU.add))
                o("vector", lambda v: v.tensor_reduce(out=SS2[:, 1:2], in_=SROW[0:1, 128:256],
                                                      axis=AX.X, op=ALU.add))
                o("vector", lambda v: v.drain(), inc="V")
                o("vector", lambda v: v.tensor_single_scalar(
                    out=SS2[:, 0:2], in_=SS2[:, 0:2], scalar=float(DFF * DM) * 1e-5, op=ALU.max))
                o("vector", lambda v: v.drain(), inc="V")
                o("scalar", lambda s: s.activation(out=SS2[:, 2:4], in_=SS2[:, 0:2], func=AF.Ln,
                                                   scale=1.0 / float(DFF * DM)),
                  wait=[("V", n("V"))])
                o("scalar", lambda s: s.activation(out=SS2[:, 2:4], in_=SS2[:, 2:4],
                                                   func=AF.Exp, scale=-0.5), inc="S")
                o("vector", lambda v: v.reciprocal(SS2[:, 4:6], SS2[:, 2:4]),
                  wait=[("S", n("S"))])
                o("vector", lambda v: v.drain(), inc="V")
                o("tensor", lambda t: t.matmul(pb[4][:, 0:4], ONEF[:], SS2[:, 2:6],
                                               start=True, stop=True),
                  wait=[("V", n("V"))] + bw("pb4"), inc="T")
                o("vector", lambda v: v.tensor_copy(out=RMS[:], in_=pb[4][:, 0:4]),
                  wait=[("T", n("T"))], inc="V")
                br("pb4", "V")
                v_rms = n("V")
                dump("d_sacc", SACC[:], [("V", v_rms)])
                dump("d_ss2", SS2[:], [("V", v_rms)])

            y_trans_T = 0
            for tc in range(NCH):
                sl = slice(tc * CH, (tc + 1) * CH)
                o("vector", lambda v, sl=sl: v.tensor_scalar_mul(AAC[:], DT[:, sl], ACOL[:]),
                  wait=[("V", vB), ("S", sB)])
                o("vector", lambda v: v.tensor_tensor_scan(
                    out=CA[:], data0=ONET[:], data1=AAC[:], initial=0.0,
                    op0=ALU.mult, op1=ALU.add))
                o("vector", lambda v: v.drain(), inc="V")
                v_ca = n("V")
                o("tensor", lambda t: t.transpose(pb[4][:, 0:64], CA[:], IDF[0:64, 0:64]),
                  wait=[("V", v_ca)] + bw("pb4"), inc="T")
                t_cat = n("T")
                o("vector", lambda v: v.tensor_copy(out=CAT[:], in_=pb[4][:, 0:64]),
                  wait=[("T", t_cat)])
                br("pb4", "V")
                o("vector", lambda v: v.tensor_scalar_mul(NCAT[:], CAT[:], -1.0), inc="V")
                v_cat = n("V")
                o("scalar", lambda s: s.activation(out=WALL, in_=CA[:], func=AF.Exp,
                                                   scale=-1.0, bias=CA[:, 127:128]),
                  wait=[("V", v_ca)], inc="S")
                s_w = n("S")
                o("tensor", lambda t: t.transpose(pb[4][:, 64:128], WALL, IDF[0:64, 0:64]),
                  wait=[("S", s_w), ("V", v_cat)] + bw("pb4"), inc="T")
                o("vector", lambda v: v.tensor_copy(out=WTM[:], in_=pb[4][:, 64:128]),
                  wait=[("T", n("T"))], inc="V")
                br("pb4", "V")
                o("scalar", lambda s: s.activation(out=ECAT[:], in_=CAT[:], func=AF.Exp),
                  wait=[("V", v_cat)], inc="S")
                o("vector", lambda v: v.tensor_scalar_mul(LTMP, ONET[:, 0:64], CA[:, 127:128]),
                  wait=[("V", v_ca)], inc="V")
                v_lt = n("V")
                o("tensor", lambda t: t.transpose(pb[4][0:64, 128:192], LTMP, IDF[0:64, 0:64]),
                  wait=[("V", v_lt)] + bw("pb4"), inc="T")
                o("scalar", lambda s: s.activation(out=LREP[:], in_=pb[4][0:64, 128:192],
                                                   func=AF.Exp),
                  wait=[("T", n("T"))], inc="S")
                br("pb4", "S")
                s_lrep = n("S")
                o("tensor", lambda t, sl=sl: t.transpose(pt[0][:, 0:64], BCb[0:64, sl], IDB[0:64, 0:64]),
                  wait=[("V", vB)] + bw("pt0"), inc="T")
                o("vector", lambda v: v.tensor_copy(out=BTM[:], in_=pt[0][:, 0:64]),
                  wait=[("T", n("T"))], inc="V")
                br("pt0", "V")
                o("tensor", lambda t, sl=sl: t.matmul(pb[4][:, 256:384], BCb[0:64, sl],
                                                      C0b[:, sl], start=True, stop=True),
                  wait=[("V", vB), ("dT", c0b_mark)] + bw("pb4"), inc="T")
                o("vector", lambda v: v.tensor_copy(out=MT[:], in_=pb[4][:, 256:384]),
                  wait=[("T", n("T"))], inc="V")
                br("pb4", "V")
                o("tensor", lambda t, sl=sl: t.transpose(pb[4][:, 192:256], DT[:, sl],
                                                         IDF[0:64, 0:64]),
                  wait=[("V", vB), ("S", s_lrep)] + bw("pb4"), inc="T")
                o("vector", lambda v: v.tensor_copy(out=DTTM[:], in_=pb[4][:, 192:256]),
                  wait=[("T", n("T"))], inc="V")
                br("pb4", "V")
                for f in range(4):
                    bname = "pt%d" % (f % 2)
                    bank = pt[f % 2]
                    for jj in range(8):
                        j = f * 8 + jj
                        o("tensor", lambda t, j=j, jj=jj, bank=bank, tc=tc: t.transpose(
                            bank[:, jj * 128:(jj + 1) * 128],
                            H[:, (32 + j) * T + tc * CH:(32 + j) * T + (tc + 1) * CH], IDB[:]),
                          wait=([("V", vB)] + bw(bname)) if jj == 0 else (),
                          inc="T" if jj == 7 else None)
                    t_f = n("T")
                    o("vector", lambda v, f=f, bank=bank: v.tensor_copy(
                        out=XT[:, f * 1024:(f + 1) * 1024], in_=bank[:]),
                      wait=[("T", t_f)], inc="V")
                    br(bname, "V")
                o("vector", lambda v: v.tensor_tensor(
                    out=DTX.rearrange("p (h q) -> p h q", h=64),
                    in0=XT.rearrange("p (h q) -> p h q", h=64),
                    in1=DTTM[:, :, None].broadcast_to([128, 64, 64]), op=ALU.mult), inc="V")
                v_shared = n("V")
                if p == 0 and tc == 0:
                    w0 = [("V", v_shared), ("S", s_lrep)]
                    dump("d_ca", CA[:], w0)
                    dump("d_cat", CAT[:], w0)
                    dump("d_ecat", ECAT[:], w0)
                    dump("d_lrep", LREP[:], w0)
                    dump("d_wtm", WTM[:], w0)
                    dump("d_mt", MT[:], w0)
                    dump("d_xt", XT, w0)
                    dump("d_dtx", DTX, w0)

                sG = {}
                tR = {}
                vG = {}
                tI = {}
                vC = {}
                for grp in range(8):
                    hs = list(range(grp * 8, grp * 8 + 8))
                    o("vector", lambda v, grp=grp: v.tensor_tensor(
                        out=BW.rearrange("p (i q) -> p i q", i=8),
                        in0=BTM[:, None, :].broadcast_to([128, 8, 64]),
                        in1=WTM[:, grp * 8:(grp + 1) * 8, None].broadcast_to([128, 8, 64]),
                        op=ALU.mult),
                      wait=([("V", v_shared), ("T", vC.get(("tg", grp - 1), 0))]
                            if grp > 0 else [("V", v_shared)]),
                      inc="V")
                    v_bw = n("V")
                    for h in hs:
                        qs = QTB[:, (h % 4) * 128:(h % 4 + 1) * 128]
                        o("vector", lambda v, qs=qs, h=h: v.tensor_scalar_add(
                            qs, NMK[:], CAT[:, h:h + 1]),
                          wait=[("T", tR.get(h - 4, 0))],
                          inc="V")
                        vq = n("V")
                        rbank = [pb[0], pb[1]][h % 2]
                        rbn = ["pb0", "pb1"][h % 2]
                        o("tensor", lambda t, qs=qs, rbank=rbank: t.transpose(
                            rbank[:, 0:128], qs, IDF[:]),
                          wait=[("V", vq)] + bw(rbn), inc="T")
                        tR[h] = n("T")
                        gs = GSB[:, (h % 4) * 128:(h % 4 + 1) * 128]
                        o("scalar", lambda s, gs=gs, rbank=rbank, h=h: s.activation(
                            out=gs, in_=rbank[:, 0:128], func=AF.Exp,
                            bias=NCAT[:, h:h + 1]),
                          wait=[("T", tR[h]), ("V", vG.get(h - 4, 0))], inc="S")
                        sG[h] = n("S")
                        br(rbn, "S")
                        gb = GTB[:, (h % 8) * 128:(h % 8 + 1) * 128]
                        o("vector", lambda v, gb=gb, gs=gs: v.tensor_tensor(
                            out=gb, in0=gs, in1=MT[:], op=ALU.mult),
                          wait=[("S", sG[h]), ("T", vC.get(("tg", grp - 1), 0))], inc="V")
                        vG[h] = n("V")
                    v_g = n("V")
                    yb = [pb[2], pb[5]][grp % 2]
                    ybn = ["pb2", "pb5"][grp % 2]
                    ib = [pb[4], ptf[0]][grp % 2]
                    ibn = ["pb4", "pt0"][grp % 2]
                    sb_ = [pb[3], ptf[1]][grp % 2]
                    sbn = ["pb3", "pt1"][grp % 2]
                    for h in hs:
                        ys = h % 8
                        o("tensor", lambda t, h=h, ys=ys, yb=yb: t.matmul(
                            yb[:, ys * 64:(ys + 1) * 64],
                            GTB[:, ys * 128:(ys + 1) * 128], DTX[:, h * 64:(h + 1) * 64],
                            start=(ys == 0), stop=(ys == 7), skip_group_check=True),
                          wait=([("V", v_g)] + bw(ybn)) if ys == 0 else (), inc="T")
                        o("tensor", lambda t, h=h, ys=ys, sb_=sb_: t.matmul(
                            sb_[0:64, ys * 64:(ys + 1) * 64],
                            BW[:, ys * 64:(ys + 1) * 64], DTX[:, h * 64:(h + 1) * 64],
                            start=(ys == 0), stop=(ys == 7), skip_group_check=True),
                          wait=([("V", v_bw)] + bw(sbn)) if ys == 0 else (), inc="T")
                    for h in hs:
                        ys = h % 8
                        o("tensor", lambda t, h=h, ys=ys, sl=sl, ib=ib: t.matmul(
                            ib[:, ys * 64:(ys + 1) * 64], C0b[:, sl],
                            S_[0:64, h * 64:(h + 1) * 64],
                            start=(ys == 0), stop=(ys == 7), skip_group_check=True),
                          wait=([("V", v_shared), ("dT", c0b_mark)] + bw(ibn))
                          if ys == 0 else (), inc="T")
                        tI[h] = n("T")
                    t_grp = n("T")
                    vC[("tg", grp)] = t_grp
                    for h in hs:
                        ys = h % 8
                        o("vector", lambda v, h=h, ys=ys, yb=yb: v.tensor_copy(
                            out=YTM[:, h * 64:(h + 1) * 64],
                            in_=yb[:, ys * 64:(ys + 1) * 64]),
                          wait=[("T", t_grp), ("S", s_lrep), ("T", y_trans_T)])
                        o("vector", lambda v, h=h, ys=ys, ib=ib: v.scalar_tensor_tensor(
                            out=YTM[:, h * 64:(h + 1) * 64],
                            in0=ib[:, ys * 64:(ys + 1) * 64],
                            scalar=ECAT[:, h:h + 1], in1=YTM[:, h * 64:(h + 1) * 64],
                            op0=ALU.mult, op1=ALU.add))
                        o("vector", lambda v, h=h, ys=ys, sb_=sb_: v.scalar_tensor_tensor(
                            out=S_[0:64, h * 64:(h + 1) * 64],
                            in0=S_[0:64, h * 64:(h + 1) * 64], scalar=LREP[:, h:h + 1],
                            in1=sb_[0:64, ys * 64:(ys + 1) * 64],
                            op0=ALU.mult, op1=ALU.add),
                          inc="V" if h == hs[-1] else None)
                    vC[grp] = n("V")
                    br(ybn, "V")
                    br(sbn, "V")
                    br(ibn, "V")
                v_heads = n("V")

                o("vector", lambda v: v.tensor_tensor(
                    out=DTX.rearrange("p (h q) -> p h q", h=64),
                    in0=XT.rearrange("p (h q) -> p h q", h=64),
                    in1=DREP[:, :, None].broadcast_to([128, 64, 64]), op=ALU.mult))
                o("vector", lambda v: v.tensor_tensor(out=YTM[:], in0=YTM[:], in1=DTX,
                                                      op=ALU.add), inc="V")
                v_y = n("V")
                o("scalar", lambda s: s.activation(out=DTX, in_=YTM[:], func=AF.Square,
                                                   accum_out=LNC[:, 1:2]),
                  wait=[("V", v_y)])
                o("scalar", lambda s: s.activation(out=LNC[:, 2:3], in_=LNC[:, 1:2], func=AF.Ln,
                                                   scale=1.0 / INTER, bias=CE6[:]))
                o("scalar", lambda s: s.activation(out=LNC[:, 2:3], in_=LNC[:, 2:3], func=AF.Exp,
                                                   scale=-0.5), inc="S")
                s_nrm = n("S")
                o("vector", lambda v: v.tensor_scalar_mul(YTM[:], YTM[:], LNC[:, 2:3]),
                  wait=[("S", s_nrm)], inc="V")
                v_yn = n("V")
                if p == 0 and tc == 0:
                    dump("d_ytm", YTM[:], [("V", v_yn)])
                    dump("d_s", S_[:], [("V", v_yn)])
                for f in range(4):
                    bname = "pt%d" % (f % 2)
                    bank = pt[f % 2]
                    for jj in range(8):
                        j = f * 8 + jj
                        o("tensor", lambda t, j=j, jj=jj, bank=bank: t.transpose(
                            bank[:, jj * 128:(jj + 1) * 128],
                            YTM[:, j * 128:(j + 1) * 128], IDB[:]),
                          wait=([("V", v_yn)] + bw(bname)) if jj == 0 else (),
                          inc="T" if jj == 7 else None)
                    t_f = n("T")
                    for jj in range(8):
                        j = f * 8 + jj
                        o("vector", lambda v, j=j, jj=jj, tc=tc, bank=bank:
                          v.scalar_tensor_tensor(
                            out=YG[:, j * T + tc * CH:j * T + (tc + 1) * CH],
                            in0=bank[:, jj * 128:(jj + 1) * 128],
                            scalar=NWC[:, j:j + 1],
                            in1=H[:, j * T + tc * CH:j * T + (tc + 1) * CH],
                            op0=ALU.mult, op1=ALU.mult),
                          wait=[("T", t_f)] if jj == 0 else (),
                          inc="V" if jj == 7 else None)
                    br(bname, "V")
                y_trans_T = n("T")
            c0b_freeT = n("T")
            vCend = n("V")
            sCend = n("S")
            if p == 0:
                dump("d_yg", YG[:, 0:2 * T], [("V", vCend)])
                dump("d_rms", RMS[:], [("V", vCend), ("V", v_rms)])
                dump("d_sc1", SC1[:], [("V", vCend)])

            for tc in range(NCH):
                o("tensor", lambda t, tc=tc: t.transpose(
                    pb[4][0:1, tc * 128:(tc + 1) * 128], SC1[:, tc:tc + 1], IDF[:]),
                  wait=([("V", vCend), ("S", sCend), ("V", v_rms)] + bw("pb4")) if tc == 0 else (),
                  inc="T")
            o("vector", lambda v: v.tensor_copy(out=SROW[:], in_=pb[4][0:1, 0:512]),
              wait=[("T", n("T"))], inc="V")
            br("pb4", "V")
            o("tensor", lambda t: t.matmul(pb[4][:, 0:512], ONEF[:], SROW[:],
                                           start=True, stop=True),
              wait=[("V", n("V"))] + bw("pb4"), inc="T")
            o("vector", lambda v: v.tensor_copy(out=R1B, in_=pb[4][:, 0:512]),
              wait=[("T", n("T"))])
            br("pb4", "V")
            o("vector", lambda v: v.drain())
            o("vector", lambda v: v.reciprocal(R1B, R1B))
            o("vector", lambda v: v.drain())
            o("vector", lambda v: v.tensor_scalar_mul(R1B, R1B, RMS[:, 2:3]))
            o("vector", lambda v: v.drain(), inc="V")
            v_r1 = n("V")

            f1_mm = {-3: t_wb_free, -2: t_wb_free, -1: t_wb_free}
            f1_epi_S = {}
            vq_prev = 0
            for m in range(NM_F1):
                half = m % 3
                wbh = WB[:, half * 2048:(half + 1) * 2048]
                wt_guard = [("T", f1_mm[m - 3])]
                if p == 0:
                    for hh in range(2):
                        shalf = SCR[:, hh * 1024:(hh + 1) * 1024]
                        dwait = [("V", vq_prev)]
                        if hh == 0:
                            dwait += wt_guard
                            if m == 0:
                                dwait += [("V", vCend)]
                        o("sync", dma(shalf, w1_d[m][:, hh * 1024:(hh + 1) * 1024]),
                          wait=dwait, inc="dW", k=16)
                        dcnt = n("dW")
                        wde = ([dC_mark[("w1", m - 3)]]
                               if (hh == 0 and ("w1", m - 3) in dC_mark) else [])
                        o("vector", lambda v, shalf=shalf: v.tensor_scalar(
                            out=shalf.bitcast(I32), in0=shalf, scalar1=RMS[:, 0:1],
                            scalar2=None, op0=ALU.mult),
                          wait=[("dW", dcnt), ("V", v_r1)] + wde)
                        o("vector", lambda v, shalf=shalf, wbh=wbh, hh=hh: v.tensor_scalar(
                            out=wbh[:, hh * 1024:(hh + 1) * 1024], in0=shalf.bitcast(I32),
                            scalar1=1.0, scalar2=-1.0, op0=ALU.min, op1=ALU.max), inc="V")
                    v_w = n("V")
                    vq_prev = v_w
                    o("gpsimd", dma(cw1_d[m], wbh), wait=[("V", v_w)], inc="dC", k=16)
                    dC_mark[("w1", m)] = ("dC", n("dC"))
                    w_ready = [("V", v_w)]
                else:
                    o("sync", dma(wbh, cw1_d[m]), wait=wt_guard, inc="dW", k=16)
                    w_ready = [("dW", n("dW"))]
                bname = "pb%d" % (m % 4)
                bank = pb[m % 4]
                for k in range(NK):
                    o("tensor", lambda t, bank=bank, wbh=wbh, k=k: t.matmul(
                        bank[:], wbh[:, k * 128:(k + 1) * 128], XQ[:, k * T:(k + 1) * T],
                        start=(k == 0), stop=(k == NK - 1)),
                      wait=(w_ready + bw(bname) + [("V", vCend), ("V", v_r1)]) if k == 0 else (),
                      inc="T" if k == NK - 1 else None)
                f1_mm[m] = n("T")
                o("vector", lambda v, bank=bank: v.tensor_tensor(out=M1[:], in0=bank[:],
                                                                 in1=R1B, op=ALU.mult),
                  wait=[("T", f1_mm[m]), ("S", f1_epi_S.get(m - 1, 0))], inc="V")
                br(bname, "V")
                o("scalar", lambda s, m=m: s.activation(out=H[:, m * T:(m + 1) * T], in_=M1[:],
                                                        func=AF.Gelu),
                  wait=[("V", n("V"))], inc="S")
                f1_epi_S[m] = n("S")
            t_wb_free = f1_mm[NM_F1 - 1]
            sD = n("S")
            if p == 0:
                dump("d_r1b", R1B, [("V", v_r1)])
                dump("d_h", H[:, 0:2 * T], [("S", sD)])

            sAbs = 0
            for j in range(64):
                o("scalar", lambda s, j=j: s.activation(out=CIN[:, 0:512],
                                                        in_=H[:, j * T:(j + 1) * T],
                                                        func=AF.Abs),
                  wait=([("S", sD), ("V", 0)] if j == 0 else [("V", sAbs)]), inc="S")
                sA = n("S")
                if j == 0:
                    o("vector", lambda v: v.tensor_copy(out=M1[:], in_=CIN[:, 0:512]),
                      wait=[("S", sA)], inc="V")
                else:
                    o("vector", lambda v: v.tensor_tensor(out=M1[:], in0=M1[:],
                                                          in1=CIN[:, 0:512], op=ALU.max),
                      wait=[("S", sA)], inc="V")
                sAbs = n("V")
            o("vector", lambda v: v.drain(), inc="V")
            v_m1 = n("V")
            for tc in range(NCH):
                o("tensor", lambda t, tc=tc: t.transpose(
                    pb[4][:, tc * 128:(tc + 1) * 128], M1[:, tc * 128:(tc + 1) * 128], IDF[:]),
                  wait=([("V", v_m1)] + bw("pb4")) if tc == 0 else (), inc="T")
            t_m1 = n("T")
            for tc in range(NCH):
                o("vector", lambda v, tc=tc: v.tensor_reduce(
                    out=SC2C[:, tc:tc + 1], in_=pb[4][:, tc * 128:(tc + 1) * 128],
                    axis=AX.X, op=ALU.max),
                  wait=[("T", t_m1)])
            o("vector", lambda v: v.drain(), inc="V")
            br("pb4", "V")
            o("vector", lambda v: v.tensor_single_scalar(out=SC2C[:, 0:4], in_=SC2C[:, 0:4],
                                                         scalar=1e-5, op=ALU.max))
            o("vector", lambda v: v.drain())
            o("vector", lambda v: v.reciprocal(SC2C[:, 4:8], SC2C[:, 0:4]))
            o("vector", lambda v: v.drain())
            o("vector", lambda v: v.tensor_single_scalar(out=SC2C[:, 4:8], in_=SC2C[:, 4:8],
                                                         scalar=127.0, op=ALU.mult))
            o("vector", lambda v: v.tensor_scalar(out=SC2C[:, 0:4], in0=SC2C[:, 0:4],
                                                  scalar1=1.0 / 127.0, scalar2=RMS[:, 3:4],
                                                  op0=ALU.mult, op1=ALU.mult))
            o("vector", lambda v: v.drain(), inc="V")
            v_sc2 = n("V")
            for src0, dstB in [(4, S2B), (0, R2B)]:
                for tc in range(NCH):
                    o("tensor", lambda t, tc=tc, src0=src0: t.transpose(
                        pb[4][0:1, tc * 128:(tc + 1) * 128], SC2C[:, src0 + tc:src0 + tc + 1],
                        IDF[:]),
                      wait=([("V", v_sc2)] + bw("pb4")) if tc == 0 else (), inc="T")
                t_r = n("T")
                o("vector", lambda v: v.tensor_copy(out=SROW[:], in_=pb[4][0:1, 0:512]),
                  wait=[("T", t_r)], inc="V")
                br("pb4", "V")
                o("tensor", lambda t: t.matmul(pb[4][:, 0:512], ONEF[:], SROW[:],
                                               start=True, stop=True),
                  wait=[("V", n("V"))] + bw("pb4"), inc="T")
                o("vector", lambda v, dstB=dstB: v.tensor_copy(out=dstB, in_=pb[4][:, 0:512]),
                  wait=[("T", n("T"))], inc="V")
                br("pb4", "V")
            v_s2b = n("V")
            for j in range(64):
                o("vector", lambda v, j=j: v.tensor_tensor(
                    out=M1[:].bitcast(I32), in0=H[:, j * T:(j + 1) * T], in1=S2B,
                    op=ALU.mult),
                  wait=[("V", v_s2b)] if j == 0 else ())
                o("vector", lambda v, j=j: v.tensor_tensor(
                    out=H[:, j * T:(j + 1) * T], in0=M1[:].bitcast(I32), in1=R2B,
                    op=ALU.mult), inc="V" if j == 63 else None)
            vE = n("V")
            if p == 0:
                dump("d_hq", H[:, 0:2 * T], [("V", vE)])
                dump("d_sc2", SC2C[:], [("V", vE)])

            fbank = [pb[0][:], pb[1][:], pb[2][:], pb[3][:], pb[4][:], pb[5][:],
                     ptf[0][:, 0:512], ptf[1][:, 0:512]]
            fbn = ["pb0", "pb1", "pb2", "pb3", "pb4", "pb5", "pt0", "pt1"]
            vq_prev = 0
            for pair in range(2):
                tcs = (2 * pair, 2 * pair + 1)
                tK = {-3: t_wb_free, -2: t_wb_free, -1: t_wb_free}
                for K in range(96):
                    half = K % 3
                    wbh = WB[:, half * 2048:(half + 1) * 2048]
                    wt_guard = [("T", tK[K - 3])]
                    if K < 32:
                        src_f = wo_d[K * 128:(K + 1) * 128, :]
                        src_c = cwo_d[K * 128:(K + 1) * 128, :]
                        quant = False
                    else:
                        r = K - 32
                        src_f = w2_d[r * 128:(r + 1) * 128, :]
                        src_c = cw2_d[r * 128:(r + 1) * 128, :]
                        quant = True
                    if p == 0 and pair == 0:
                        if quant:
                            for hh in range(2):
                                shalf = SCR[:, hh * 1024:(hh + 1) * 1024]
                                dwait = [("V", vq_prev)]
                                if hh == 0:
                                    dwait += wt_guard
                                    if K == 32:
                                        dwait += [("V", vE)] + scr_free
                                o("sync", dma(shalf, src_f[:, hh * 1024:(hh + 1) * 1024]),
                                  wait=dwait, inc="dW", k=16)
                                dcnt = n("dW")
                                wde = ([dC_mark[("w2", K - 3)]]
                                       if (hh == 0 and ("w2", K - 3) in dC_mark) else [])
                                o("vector", lambda v, shalf=shalf: v.tensor_scalar(
                                    out=shalf.bitcast(I32), in0=shalf, scalar1=RMS[:, 1:2],
                                    scalar2=None, op0=ALU.mult),
                                  wait=[("dW", dcnt), ("V", vE)] + wde)
                                o("vector", lambda v, shalf=shalf, wbh=wbh, hh=hh:
                                  v.tensor_scalar(
                                    out=wbh[:, hh * 1024:(hh + 1) * 1024],
                                    in0=shalf.bitcast(I32),
                                    scalar1=1.0, scalar2=-1.0, op0=ALU.min, op1=ALU.max),
                                  inc="V")
                            v_w = n("V")
                            vq_prev = v_w
                            o("gpsimd", dma(src_c, wbh), wait=[("V", v_w)], inc="dC", k=16)
                            dC_mark[("w2", K)] = ("dC", n("dC"))
                            w_ready = [("V", v_w)]
                        else:
                            wde = [dC_mark[("wo", K - 3)]] if ("wo", K - 3) in dC_mark else []
                            o("gpsimd", dma(wbh, src_f), wait=wt_guard + wde, inc="dT", k=16)
                            w_ready = [("dT", n("dT"))]
                            o("gpsimd", dma(src_c, wbh), wait=w_ready, inc="dC", k=16)
                            dC_mark[("wo", K)] = ("dC", n("dC"))
                    else:
                        wcache = ([dC_mark[("w2" if quant else "wo", K)]]
                                  if p == 0 and pair == 1 else [])
                        o("sync", dma(wbh, src_c), wait=wt_guard + wcache, inc="dW", k=16)
                        w_ready = [("dW", n("dW"))]
                    stat_buf = YG if K < 32 else H
                    koff = K if K < 32 else K - 32
                    bank_guard = sum([bw(b) for b in fbn], []) if K == 0 else []
                    for ci, tc in enumerate(tcs):
                        for ns in range(4):
                            bidx = ci * 4 + ns
                            o("tensor", lambda t, stat_buf=stat_buf, koff=koff, tc=tc,
                              bidx=bidx, ns=ns, wbh=wbh, K=K: t.matmul(
                                fbank[bidx][:, 0:512],
                                stat_buf[:, koff * T + tc * CH:koff * T + (tc + 1) * CH],
                                wbh[:, ns * 512:(ns + 1) * 512],
                                start=(K == 0), stop=(K == 95), skip_group_check=True),
                              wait=(w_ready + [("V", vE)] + bank_guard)
                              if (ci == 0 and ns == 0) else (),
                              inc="T" if (ci == 1 and ns == 3) else None)
                    tK[K] = n("T")
                tF = n("T")
                t_wb_free = tF
                if dbg and p == 0 and pair == 0:
                    o("vector", lambda v: v.tensor_copy(out=M1[:], in_=pb[0][:]),
                      wait=[("T", tF)], inc="V")
                    dump("d_ps0", M1[:], [("V", n("V"))])
                    o("vector", lambda v: v.tensor_copy(out=M1[:], in_=pb[2][:]),
                      wait=[("V", 0)], inc="V")
                    dump("d_ps2", M1[:], [("V", n("V"))])
                for ci, tc in enumerate(tcs):
                    g = p * NCH + tc
                    o("scalar", dma(XIN[:], x_d[g * CH:(g + 1) * CH, :]),
                      wait=xin_free, inc="dX", k=16)
                    d_x = n("dX")
                    for ns in range(4):
                        bidx = ci * 4 + ns
                        o("vector", lambda v, bidx=bidx, ns=ns: v.scalar_tensor_tensor(
                            out=SCR[:, ns * 512:(ns + 1) * 512],
                            in0=fbank[bidx][:, 0:512], scalar=1.0,
                            in1=XIN[:, ns * 512:(ns + 1) * 512],
                            op0=ALU.mult, op1=ALU.add),
                          wait=([("T", tF), ("dX", d_x)] + scr_free) if ns == 0 else (),
                          inc="V" if ns == 3 else None)
                        br(fbn[bidx], "V")
                    v_out = n("V")
                    xin_free = [("V", v_out)]
                    o("gpsimd", dma(out_d[g * CH:(g + 1) * CH, :], SCR[:]),
                      wait=[("V", v_out)], inc="dO", k=16)
                    scr_free = [("dO", n("dO"))]

        o("gpsimd", None, wait=[("dO", em.cnt["dO"]), ("dC", em.cnt["dC"])])
        em.finalize(block)
    return nc


def _prep_inputs(inputs):
    x = np.ascontiguousarray(inputs["L0_attn_hidden_states"])
    wip = inputs["L0_attn_in_proj_w"]
    wipp = np.zeros((PROJP, DM), np.float32)
    wipp[:8384] = wip
    wtile = np.ascontiguousarray(
        wipp.reshape(NM_IP, 128, NK, 128).transpose(0, 3, 2, 1)).reshape(NM_IP, 128, NK * 128)
    w1tile = np.ascontiguousarray(
        np.asarray(inputs["L0_ffn_w1"]).reshape(NM_F1, 128, NK, 128).transpose(0, 3, 2, 1)
    ).reshape(NM_F1, 128, NK * 128)
    w2t = np.ascontiguousarray(np.asarray(inputs["L0_ffn_w2"]).T)
    wot = np.ascontiguousarray(np.asarray(inputs["L0_attn_out_proj_w"]).T)
    convw = np.ascontiguousarray(
        np.asarray(inputs["L0_attn_conv_w"]).reshape(CONVCH, 128, 4).transpose(1, 0, 2)
    ).reshape(128, CONVCH * 4)
    convb = np.ascontiguousarray(np.asarray(inputs["L0_attn_conv_b"]).reshape(CONVCH, 128).T)
    dtb = np.ascontiguousarray(np.asarray(inputs["L0_attn_dt_bias"]).reshape(64, 1))
    alog = np.ascontiguousarray(np.asarray(inputs["L0_attn_A_log"]).reshape(64, 1))
    dvec = np.ascontiguousarray(np.asarray(inputs["L0_attn_D"]).reshape(1, 64))
    nwc = np.ascontiguousarray(np.asarray(inputs["L0_attn_norm_weight"]).reshape(32, 128).T)
    identf = np.eye(128, dtype=np.float32)
    identb = np.eye(128, dtype=np.float32).astype(ml_dtypes.bfloat16)
    rows = np.arange(128)
    nmask = np.where(rows[:, None] >= rows[None, :], 0.0, -1e9).astype(np.float32)
    onesf = np.ones((1, 128), np.float32)
    shared = dict(wip=wtile, w1t=w1tile, w2t=w2t, wot=wot, convw=convw, convb=convb,
                  dtb=dtb, alog=alog, dvec=dvec, nwc=nwc, identf=identf, identb=identb,
                  nmask=nmask, onesf=onesf)
    return x, shared


def kernel(**inputs):
    x, shared = _prep_inputs(inputs)
    nc = build()
    in_maps = [dict(x=np.ascontiguousarray(np.asarray(x[b], np.float32)), **shared)
               for b in range(8)]
    res = run_bass_kernel_spmd(nc, in_maps, core_ids=list(range(8)))
    out = np.stack([np.asarray(r["out"]) for r in res.results], axis=0)
    return out.astype(np.float32)
